# version 36
# baseline (speedup 1.0000x reference)
import numpy as np
import concourse.bass as bass
import concourse.mybir as mybir
from concourse.bass_utils import run_bass_kernel_spmd
from contextlib import ExitStack

F32 = mybir.dt.float32
BF16 = mybir.dt.bfloat16
AF = mybir.ActivationFunctionType
ALU = mybir.AluOpType
AX = mybir.AxisListType

T, D = 2048, 1024
NTT, NDC = 16, 8
M, INNER, MH = 64, 16, 1024
CW = [T - 128 * jc for jc in range(16)]
COFF = [0]
for w in CW:
    COFF.append(COFF[-1] + w)
PTOT = COFF[-1]


def _slices512(w):
    out = []
    off = 0
    while off < w:
        out.append((off, min(512, w - off)))
        off += 512
    return out


def build_nc(debug=False):
    nc = bass.Bass("TRN2", target_bir_lowering=False)

    x_h = nc.dram_tensor("x", [T, D], F32, kind="ExternalInput").ap()
    xt_h = nc.dram_tensor("xt", [D, T], F32, kind="ExternalInput").ap()
    wq_h = nc.dram_tensor("wq", [D, D], F32, kind="ExternalInput").ap()
    wk_h = nc.dram_tensor("wk", [D, D], F32, kind="ExternalInput").ap()
    wv_h = nc.dram_tensor("wv", [D, D], F32, kind="ExternalInput").ap()
    bqT_h = nc.dram_tensor("bqT", [128, 8], F32, kind="ExternalInput").ap()
    bkT_h = nc.dram_tensor("bkT", [128, 8], F32, kind="ExternalInput").ap()
    bvb_h = nc.dram_tensor("bvb", [128, D], F32, kind="ExternalInput").ap()
    win_h = nc.dram_tensor("win", [D, M], F32, kind="ExternalInput").ap()
    binc_h = nc.dram_tensor("binc", [M, 1], F32, kind="ExternalInput").ap()
    w1e_h = nc.dram_tensor("w1e", [M, MH], F32, kind="ExternalInput").ap()
    b1T_h = nc.dram_tensor("b1T", [128, 8], F32, kind="ExternalInput").ap()
    w2eT_h = nc.dram_tensor("w2eT", [MH, M], F32, kind="ExternalInput").ap()
    b2c_h = nc.dram_tensor("b2c", [M, 1], F32, kind="ExternalInput").ap()
    wout_h = nc.dram_tensor("wout", [M, D], F32, kind="ExternalInput").ap()
    boutr_h = nc.dram_tensor("boutr", [1, D], F32, kind="ExternalInput").ap()
    mask_h = nc.dram_tensor("mask", [128, 128], F32, kind="ExternalInput").ap()
    z_h = nc.dram_tensor("z", [T, D], F32, kind="ExternalOutput").ap()
    abf_h = nc.dram_tensor("abf", [T, D], BF16, kind="Internal").ap()

    left = ExitStack()
    right = ExitStack()

    def sb(st, name, shape, dtype, side):
        return st.enter_context(nc.sbuf_tensor(name, shape, dtype, side=side))

    bvb_s = sb(left, "bvb_s", [128, D], BF16, "left")
    mask_s = sb(left, "mask_s", [128, 128], F32, "left")
    bqT_s = sb(left, "bqT_s", [128, 8], F32, "left")
    bkT_s = sb(left, "bkT_s", [128, 8], F32, "left")
    S_s = sb(left, "S_s", [128, 16], F32, "left")
    negmax_s = sb(left, "negmax_s", [128, 16], F32, "left")
    R_s = sb(left, "R_s", [128, 16], F32, "left")
    ys_s = sb(left, "ys_s", [128, 16], F32, "left")
    sqs_s = sb(left, "sqs_s", [128, 16], F32, "left")
    zs0_s = sb(left, "zs0_s", [128, 16], F32, "left")
    zs1_s = sb(left, "zs1_s", [128, 16], F32, "left")
    sqs2_s = sb(left, "sqs2_s", [128, 16], F32, "left")
    negmean_s = sb(left, "negmean_s", [128, 32], F32, "left")
    m2e_s = sb(left, "m2e_s", [128, 32], F32, "left")
    veps_s = sb(left, "veps_s", [128, 32], F32, "left")
    sq_s = sb(left, "sq_s", [128, 32], F32, "left")
    rstd_s = sb(left, "rstd_s", [128, 32], F32, "left")
    negmr_s = sb(left, "negmr_s", [128, 32], F32, "left")
    junk_s = sb(left, "junk_s", [128, 1024], BF16, "left")
    win_s = sb(left, "win_s", [128, NDC * M], BF16, "left")
    w1e_s = sb(left, "w1e_s", [M, MH], BF16, "left")
    w2eT_s = sb(left, "w2eT_s", [128, NDC * M], BF16, "left")
    wout_s = sb(left, "wout_s", [M + 1, D], BF16, "left")
    b1T_s = sb(left, "b1T_s", [128, 8], F32, "left")
    binc_s = sb(left, "binc_s", [M, 1], F32, "left")
    b2c_s = sb(left, "b2c_s", [M, 1], F32, "left")
    ps = left.enter_context(nc.psum_tensor("ps", [128, 4096], F32))

    vp_scope = ExitStack()
    V_bf = vp_scope.enter_context(nc.sbuf_tensor("V_bf", [128, NTT * D], BF16, side="left"))
    probsT = vp_scope.enter_context(nc.sbuf_tensor("probsT", [128, PTOT], BF16, side="left"))
    qk_scope = ExitStack()
    Qf = qk_scope.enter_context(nc.sbuf_tensor("Qf", [128, NDC * T], BF16, side="left"))
    Kf = qk_scope.enter_context(nc.sbuf_tensor("Kf", [128, NDC * T], BF16, side="left"))
    r1_scope = ExitStack()
    xf = r1_scope.enter_context(nc.sbuf_tensor("xf", [128, NDC * T], BF16, side="right"))
    wb0 = r1_scope.enter_context(nc.sbuf_tensor("wb0", [128, NDC * D], BF16, side="right"))
    wb1 = r1_scope.enter_context(nc.sbuf_tensor("wb1", [128, NDC * D], BF16, side="right"))

    sems = ExitStack()

    def sem(name):
        return sems.enter_context(nc.semaphore(name))

    sWqC = [sem(f"sWqC{c}") for c in range(NDC)]
    sWk, sWv, sBv = sem("sWk"), sem("sWv"), sem("sBv")
    sLsm, sLf, sL1 = sem("sLsm"), sem("sLf"), sem("sL1")
    sTP, sSx, sSt, sZ = sem("sTP"), sem("sSx"), sem("sSt"), sem("sZ")
    sXf = [sem(f"sXf{k}") for k in range(4)]
    sT, sV, sA = sem("sT"), sem("sV"), sem("sA")

    T_tile = lambda q: q + 1
    T_QDONE = 32
    T_QKV = 96
    T_chunk = lambda pos: 96 + pos + 1
    T_read = lambda it: 112 + it + 1
    T_READ = 128
    ffn_tiles = []
    for b in range(4):
        ffn_tiles.append(("sT", b))
        ffn_tiles += [("HT", m, b) for m in range(NDC)]
        ffn_tiles.append(("gT", b))
        ffn_tiles += [("out", 8 * b + i) for i in range(8)]
    T_ffn = {}
    bank_of = {}
    for i, key in enumerate(ffn_tiles):
        T_ffn[key] = T_READ + i + 1
        if key[0] in ("sT", "gT"):
            bank_of[key] = 6 + key[1] % 2
        elif key[0] == "HT":
            bank_of[key] = 2 + key[1] % 4
        else:
            bank_of[key] = (0, 1, 2, 3, 4, 5, 0, 1)[key[1] % 8]

    V_evac = lambda q: q + 1
    V_QKV = 96
    A_exp = lambda jc: jc + 1
    A_EXP = 16

    def _ln_order(ops_young, ops_mid, ops_old, ngrp=4):
        seq = []
        for s in range(ngrp + 2):
            if 1 <= s <= ngrp:
                seq += ops_mid(s - 1)
            if 2 <= s <= ngrp + 1:
                seq += ops_old(s - 2)
            if s < ngrp:
                seq += ops_young(s)
        return seq

    _score5 = lambda jc: [("mask", jc), ("max", jc), ("Rt", jc),
                          ("screcip", jc), ("scale", jc)]
    JCO_POS = {jc: pos for pos, jc in enumerate(
        [12, 13, 14, 15] + list(range(8)) + [8, 9, 10, 11])}
    rv_seq = _ln_order(
        lambda g: [("stt", 4 * g + i) for i in range(4) if 4 * g + i >= 2]
        + [("negmean4", g), ("m2_4", g)],
        lambda g: [("vepsa4", g), ("veps4", g)],
        lambda g: [("recip4", g), ("negmr4", g)])
    JCO_SCHED = [12, 13, 14, 15] + list(range(8)) + [8, 9, 10, 11]
    sv_seq = (sum((_score5(jc) for jc in JCO_SCHED[:15]), [])
              + [("stt", 0), ("stt", 1)]
              + _score5(JCO_SCHED[15])
              + rv_seq)
    rv = {}
    vc = V_QKV
    for k in sv_seq:
        vc += 1
        rv[k] = vc
    V_READ = vc
    V_scale = lambda jc: rv[("scale", jc)]
    ra_seq = _ln_order(
        lambda g: [("square", 4 * g + i) for i in range(4)],
        lambda g: [("sqrt4", g)],
        lambda g: [("abf", 4 * g + i) for i in range(4)])
    ra = {}
    ac = A_EXP
    for k in ra_seq:
        ac += 1
        ra[k] = ac
    A_READ = ac

    def _ffn_stages(stages, ngrp=4):
        seq = []
        for s in range(ngrp + len(stages) - 1):
            for lag, ops in enumerate(reversed(stages)):
                b = s - (len(stages) - 1 - lag)
                if 0 <= b < ngrp:
                    seq += ops(b)
        return seq

    fv_seq = [("ones", 0)] + _ffn_stages([
        lambda b: [("stev", b)] + ([("htev", (m, b)) for m in range(NDC)]
                                   if b < 2 else []),
        lambda b: [("af32", 4 * b + i) for i in range(4) if i % 2 == 0]
        + [("gtev", b)] + [("zpre", 4 * b + i) for i in range(4)]
        + [("negmean24", 2 * b), ("m2_24", 2 * b),
           ("negmean24", 2 * b + 1), ("m2_24", 2 * b + 1)],
        lambda b: [("vepsa24", 2 * b), ("veps24", 2 * b),
                   ("vepsa24", 2 * b + 1), ("veps24", 2 * b + 1)],
        lambda b: [("recip24", 2 * b), ("negmr24", 2 * b), ("norm", 4 * b),
                   ("recip24", 2 * b + 1), ("negmr24", 2 * b + 1),
                   ("norm", 4 * b + 2)]])
    fv = {}
    vc = V_READ
    for k in fv_seq:
        vc += 1
        fv[k] = vc
    V_FFN = vc
    fa_seq = _ffn_stages([
        lambda b: [("htev", (m, b)) for m in range(NDC)] if b >= 2 else [],
        lambda b: [("af32", 4 * b + i) for i in range(4) if i % 2 == 1]
        + [("square2", 4 * b + i) for i in range(4)],
        lambda b: [("sqrt24", 2 * b), ("sqrt24", 2 * b + 1)],
        lambda b: [("norm", 4 * b + 1), ("norm", 4 * b + 3)]])
    fa = {}
    ac = A_READ
    for k in fa_seq:
        ac += 1
        fa[k] = ac
    A_FFN = ac

    qkv_tiles = ([("Q", oc, nt) for nt in range(4) for oc in range(NDC)]
                 + [("K", oc, nt) for nt in range(4) for oc in range(NDC)]
                 + [("V", tt, vt) for tt in range(NTT) for vt in range(2)])

    with nc.Block() as block:
        xf3 = xf[:, :].rearrange("p (c t) -> p c t", c=NDC)
        xt3 = xt_h.rearrange("(c p) t -> p c t", p=128)
        wq3 = wq_h.rearrange("(c p) o -> p c o", p=128)
        wb03 = wb0[:, :].rearrange("p (c o) -> p c o", c=NDC)

        @block.gpsimd
        def _(g):
            g.dma_start(out=xf3[:, :, 0:512], in_=xt3[:, :, 0:512]).then_inc(sXf[0], 16)
            for c in range(4):
                g.dma_start(out=wb03[:, c:c + 1, :],
                            in_=wq3[:, c:c + 1, :]).then_inc(sWqC[c], 16)
            g.dma_start(out=xf3[:, :, 512:1024],
                        in_=xt3[:, :, 512:1024]).then_inc(sXf[1], 16)
            for c in range(4, NDC):
                g.dma_start(out=wb03[:, c:c + 1, :],
                            in_=wq3[:, c:c + 1, :]).then_inc(sWqC[c], 16)
            g.dma_start(out=xf3[:, :, 1024:1536],
                        in_=xt3[:, :, 1024:1536]).then_inc(sXf[2], 16)
            g.dma_start(out=xf3[:, :, 1536:2048],
                        in_=xt3[:, :, 1536:2048]).then_inc(sXf[3], 16)
            g.dma_start(out=wb1[:, :].rearrange("p (c o) -> p c o", c=NDC),
                        in_=wk_h.rearrange("(c p) o -> p c o", p=128)).then_inc(sWk, 16)
            g.dma_start(out=bvb_s[:, :], in_=bvb_h).then_inc(sBv, 16)
            g.wait_ge(sT, T_QDONE)
            g.dma_start(out=wb03,
                        in_=wv_h.rearrange("(c p) o -> p c o", p=128)).then_inc(sWv, 16)
            g.dma_start(out=win_s[:, :].rearrange("p (c m) -> p c m", c=NDC),
                        in_=win_h.rearrange("(c p) m -> p c m", p=128)).then_inc(sLf, 16)
            g.dma_start(out=w1e_s[:, :], in_=w1e_h).then_inc(sLf, 16)
            g.dma_start(out=w2eT_s[:, :].rearrange("p (c m) -> p c m", c=NDC),
                        in_=w2eT_h.rearrange("(c p) m -> p c m", p=128)).then_inc(sLf, 16)
            g.dma_start(out=wout_s[0:M, :], in_=wout_h).then_inc(sLf, 16)
            g.dma_start(out=b1T_s[:, :], in_=b1T_h).then_inc(sLf, 16)
            g.dma_start(out=binc_s[:, :], in_=binc_h).then_inc(sLf, 16)
            g.dma_start(out=b2c_s[:, :], in_=b2c_h).then_inc(sLf, 16)
            g.dma_start(out=wout_s[M:M + 1, :], in_=boutr_h).then_inc(sL1, 16)

        @block.sync
        def _(s):
            s.dma_start(out=bqT_s[:, :], in_=bqT_h).then_inc(sLsm, 16)
            s.dma_start(out=bkT_s[:, :], in_=bkT_h).then_inc(sLsm, 16)
            s.dma_start(out=mask_s[:, :], in_=mask_h).then_inc(sLsm, 16)

        @block.tensor
        def _(t):
            t.wait_ge(sXf[0], 16)
            for half in range(2):
                for kc in range(4 * half, 4 * half + 4):
                    t.wait_ge(sWqC[kc], 16)
                    for q in range(8):
                        ins = t.matmul(
                            ps[:, q * 512:(q + 1) * 512],
                            wb0[:, kc * D + q * 128: kc * D + q * 128 + 128],
                            xf[:, kc * T: kc * T + 512],
                            start=(kc == 0), stop=(kc == NDC - 1))
                        if kc == NDC - 1:
                            ins.then_inc(sT, 1)
            for q, (proj, a0, a1) in enumerate(qkv_tiles):
                if q < 8:
                    continue
                if proj == "Q" and a0 == 0:
                    t.wait_ge(sXf[a1], 16)
                if proj == "K" and q == 32:
                    t.wait_ge(sWk, 16)
                if proj == "V" and q == 64:
                    t.wait_ge(sWv, 16)
                wb = {"Q": wb0, "K": wb1, "V": wb0}[proj]
                bank = q % 8
                if q >= 8:
                    t.wait_ge(sV, V_evac(q - 8))
                for kc in range(NDC):
                    if proj != "V":
                        lhsT = wb[:, kc * D + a0 * 128: kc * D + a0 * 128 + 128]
                        rhs = xf[:, kc * T + a1 * 512: kc * T + a1 * 512 + 512]
                    else:
                        lhsT = xf[:, kc * T + a0 * 128: kc * T + a0 * 128 + 128]
                        rhs = wb[:, kc * D + a1 * 512: kc * D + a1 * 512 + 512]
                    ins = t.matmul(ps[:, bank * 512:(bank + 1) * 512],
                                   lhsT, rhs, start=(kc == 0), stop=(kc == NDC - 1))
                ins.then_inc(sT, 1)

        @block.vector
        def _(v):
            v.wait_ge(sLsm, 48)
            for q, (proj, a0, a1) in enumerate(qkv_tiles):
                bank = q % 8
                pslice = ps[:, bank * 512:(bank + 1) * 512]
                if proj == "V" and q == 64:
                    v.wait_ge(sBv, 16)
                v.wait_ge(sT, T_tile(q))
                if proj == "Q":
                    v.tensor_scalar(Qf[:, a0 * T + a1 * 512: a0 * T + a1 * 512 + 512],
                                    pslice, bqT_s[:, a0:a0 + 1], None,
                                    ALU.add).then_inc(sV, 1)
                elif proj == "K":
                    v.tensor_scalar(Kf[:, a0 * T + a1 * 512: a0 * T + a1 * 512 + 512],
                                    pslice, bkT_s[:, a0:a0 + 1], None,
                                    ALU.add).then_inc(sV, 1)
                else:
                    v.tensor_tensor(V_bf[:, a0 * D + a1 * 512: a0 * D + a1 * 512 + 512],
                                    pslice, bvb_s[:, a1 * 512:(a1 + 1) * 512],
                                    ALU.add).then_inc(sV, 1)

        r1_scope.close()

        JCO = [12, 13, 14, 15] + list(range(8)) + [8, 9, 10, 11]
        SPLACE = {12: (4, 1), 13: (5, 1), 14: (6, 1), 15: (7, 1),
                  0: (0, 4), 1: (4, 4), 2: (0, 4), 3: (4, 4),
                  4: (0, 3), 5: (4, 3), 6: (0, 3), 7: (4, 3),
                  8: (4, 2), 9: (6, 2), 10: (4, 2), 11: (6, 2)}
        A_expd = {jc: pos + 1 for pos, jc in enumerate(JCO)}
        bank_owner = {b: None for b in range(8)}

        def bank_gate(t, banks):
            g = {}
            for b in banks:
                o = bank_owner[b]
                if o:
                    k, val = o
                    g[k] = max(g.get(k, 0), val)
            for k, val in sorted(g.items()):
                t.wait_ge({"V": sV, "A": sA}[k], val)

        def bank_set(banks, kind, val):
            for b in banks:
                bank_owner[b] = (kind, val)

        @block.tensor
        def _(t):
            t.wait_ge(sV, V_QKV)
            for pos, jc in enumerate(JCO):
                W = CW[jc]
                b0, nb = SPLACE[jc]
                boff = b0 * 512
                bank_gate(t, range(b0, b0 + nb))
                bank_set(range(b0, b0 + nb), "A", A_expd[jc])
                for (soff, sw) in _slices512(W):
                    for kc in range(NDC):
                        ins = t.matmul(
                            ps[:, boff + soff: boff + soff + sw],
                            Kf[:, kc * T + jc * 128: kc * T + jc * 128 + 128],
                            Qf[:, kc * T + jc * 128 + soff: kc * T + jc * 128 + soff + sw],
                            start=(kc == 0), stop=(kc == NDC - 1))
                ins.then_inc(sT, 1)

        @block.scalar
        def _(a):
            for jc in JCO:
                W = CW[jc]
                boff = SPLACE[jc][0] * 512
                a.wait_ge(sV, rv[("max", jc)])
                a.activation(probsT[:, COFF[jc]:COFF[jc] + W], ps[:, boff:boff + W],
                             AF.Exp, bias=negmax_s[:, jc:jc + 1], scale=1.0,
                             accum_out=S_s[:, jc:jc + 1]).then_inc(sA, 1)

        qk_scope.close()
        stream = right.enter_context(nc.sbuf_tensor("stream", [128, NTT * D], F32, side="right"))
        a_roll_scope = ExitStack()
        a_roll = a_roll_scope.enter_context(
            nc.sbuf_tensor("a_roll", [128, 4 * D], BF16, side="right"))

        @block.sync
        def _(s):
            s.wait_ge(sT, T_QKV)
            s.dma_start(out=stream[:, :].rearrange("p (tt d) -> p tt d", tt=NTT),
                        in_=x_h.rearrange("(tt p) d -> p tt d", p=128)).then_inc(sSx, 16)

        @block.tensor
        def _(t):
            for it in range(16):
                boff = (it % 2) * 1024
                banks = (2 * (it % 2), 2 * (it % 2) + 1)
                t.wait_ge(sV, V_scale(it))
                bank_gate(t, banks)
                bank_set(banks, "V", rv[("stt", it)])
                for vt in range(2):
                    for jc in range(it + 1):
                        ins = t.matmul(
                            ps[:, boff + vt * 512: boff + vt * 512 + 512],
                            probsT[:, COFF[jc] + (it - jc) * 128: COFF[jc] + (it - jc) * 128 + 128],
                            V_bf[:, jc * D + vt * 512: jc * D + vt * 512 + 512],
                            start=(jc == 0), stop=(jc == it))
                ins.then_inc(sT, 1)

        @block.vector
        def _(v):
            sSx_waited = [False]
            for kind, i in sv_seq:
                if kind == "mask":
                    jc = i
                    boff = SPLACE[jc][0] * 512
                    v.wait_ge(sT, T_chunk(JCO_POS[jc]))
                    v.tensor_tensor(ps[:, boff:boff + 128], ps[:, boff:boff + 128],
                                    mask_s[:, :], ALU.add).then_inc(sV, 1)
                elif kind == "max":
                    jc = i
                    boff = SPLACE[jc][0] * 512
                    v.wait_ge(sV, rv[("mask", jc)])
                    v.tensor_reduce(negmax_s[:, jc:jc + 1], ps[:, boff:boff + CW[jc]],
                                    AX.X, ALU.max, negate=True).then_inc(sV, 1)
                elif kind == "Rt":
                    jc = i
                    v.wait_ge(sA, A_expd[jc])
                    v.tensor_scalar(R_s[:, jc:jc + 1], S_s[:, jc:jc + 1], 32.0, None,
                                    ALU.mult).then_inc(sV, 1)
                elif kind == "screcip":
                    jc = i
                    v.wait_ge(sV, rv[("Rt", jc)])
                    v.reciprocal(R_s[:, jc:jc + 1], R_s[:, jc:jc + 1]).then_inc(sV, 1)
                elif kind == "scale":
                    jc = i
                    v.wait_ge(sV, rv[("screcip", jc)])
                    v.tensor_scalar(probsT[:, COFF[jc]:COFF[jc] + CW[jc]],
                                    probsT[:, COFF[jc]:COFF[jc] + CW[jc]],
                                    R_s[:, jc:jc + 1], None, ALU.mult).then_inc(sV, 1)
                elif kind == "stt":
                    it = i
                    if not sSx_waited[0]:
                        v.wait_ge(sSx, 16)
                        sSx_waited[0] = True
                    boff = (it % 2) * 1024
                    so = it * D
                    v.wait_ge(sT, T_read(it))
                    v.scalar_tensor_tensor(
                        stream[:, so:so + D], stream[:, so:so + D],
                        2.0, ps[:, boff:boff + 1024], ALU.mult, ALU.add,
                        accum_out=ys_s[:, it:it + 1]).then_inc(sV, 1)
                elif kind == "negmean4":
                    c4 = slice(4 * i, 4 * i + 4)
                    v.wait_ge(sV, rv[("stt", 4 * i + 3)])
                    v.tensor_scalar(negmean_s[:, c4], ys_s[:, c4],
                                    -1.0 / D, None, ALU.mult).then_inc(sV, 1)
                elif kind == "m2_4":
                    c4 = slice(4 * i, 4 * i + 4)
                    v.wait_ge(sV, rv[("negmean4", i)])
                    v.tensor_tensor(m2e_s[:, c4], negmean_s[:, c4],
                                    negmean_s[:, c4], ALU.mult).then_inc(sV, 1)
                elif kind == "vepsa4":
                    c4 = slice(4 * i, 4 * i + 4)
                    v.wait_ge(sV, rv[("m2_4", i)])
                    v.wait_ge(sA, ra[("square", 4 * i + 3)])
                    v.scalar_tensor_tensor(veps_s[:, c4], sqs_s[:, c4],
                                           1.0 / D, m2e_s[:, c4],
                                           ALU.mult, ALU.subtract).then_inc(sV, 1)
                elif kind == "veps4":
                    c4 = slice(4 * i, 4 * i + 4)
                    v.wait_ge(sV, rv[("vepsa4", i)])
                    v.tensor_scalar(veps_s[:, c4], veps_s[:, c4], 1e-5, None,
                                    ALU.add).then_inc(sV, 1)
                elif kind == "recip4":
                    c4 = slice(4 * i, 4 * i + 4)
                    v.wait_ge(sA, ra[("sqrt4", i)])
                    v.reciprocal(rstd_s[:, c4], sq_s[:, c4]).then_inc(sV, 1)
                elif kind == "negmr4":
                    c4 = slice(4 * i, 4 * i + 4)
                    v.wait_ge(sV, rv[("recip4", i)])
                    v.tensor_tensor(negmr_s[:, c4], negmean_s[:, c4],
                                    rstd_s[:, c4], ALU.mult).then_inc(sV, 1)

        @block.scalar
        def _(a):
            for kind, i in ra_seq:
                if kind == "square":
                    it = i
                    so = it * D
                    a.wait_ge(sV, rv[("stt", it)])
                    if it >= 1:
                        a.wait_ge(sA, ra[("square", it - 1)])
                    a.activation(junk_s[:, :], stream[:, so:so + D], AF.Square,
                                 accum_out=sqs_s[:, it:it + 1]).then_inc(sA, 1)
                elif kind == "sqrt4":
                    g = i
                    a.wait_ge(sV, rv[("veps4", g)])
                    a.activation(sq_s[:, 4 * g:4 * g + 4], veps_s[:, 4 * g:4 * g + 4],
                                 AF.Sqrt).then_inc(sA, 1)
                elif kind == "abf":
                    it = i
                    so = it * D
                    slot = (it % 4) * D
                    if it >= 4:
                        a.wait_ge(sTP, 16 * (it // 4))
                    a.wait_ge(sV, rv[("negmr4", it // 4)])
                    a.activation(a_roll[:, slot:slot + D], stream[:, so:so + D],
                                 AF.Identity, bias=negmr_s[:, it:it + 1],
                                 scale=rstd_s[:, it:it + 1]).then_inc(sA, 1)

        af_scope = ExitStack()
        af = af_scope.enter_context(nc.sbuf_tensor("af", [128, NDC * T], BF16, side="right"))
        af3 = af[:, :].rearrange("p (c t) -> p c t", c=NDC)

        @block.sync
        def _(s):
            s.wait_ge(sWv, 16)
            s.wait_ge(sLf, 16 * 7)
            s.wait_ge(sL1, 16)
            s.wait_ge(sSx, 16)
            for g in range(4):
                for it in range(4 * g, 4 * g + 4):
                    s.wait_ge(sA, ra[("abf", it)])
                    s.dma_start(out=abf_h[it * 128:(it + 1) * 128, :],
                                in_=a_roll[:, (it % 4) * D:(it % 4) * D + D]
                                ).then_inc(sSt, 16)
                s.wait_ge(sSt, 16 * (4 * g + 4))
                s.dma_start(out=af3[:, :, 512 * g:512 * (g + 1)],
                            in_=abf_h[512 * g:512 * (g + 1), :],
                            transpose=True).then_inc(sTP, 16)

        vp_scope.close()
        ffn = ExitStack()
        HT = ffn.enter_context(nc.sbuf_tensor("HT", [128, NDC * T], BF16, side="left"))
        sT_s = ffn.enter_context(nc.sbuf_tensor("sT_s", [M, T], BF16, side="left"))
        gT_s = ffn.enter_context(nc.sbuf_tensor("gT_s", [M + 1, T], BF16, side="left"))

        @block.tensor
        def _(t):
            first_out = True
            for key in ffn_tiles:
                bank = bank_of[key]
                bs = ps[:, bank * 512:(bank + 1) * 512]
                bs64 = ps[0:M, bank * 512:bank * 512 + 512]
                if key[0] == "sT":
                    b = key[1]
                    t.wait_ge(sTP, 16 * (b + 1))
                    bank_gate(t, (bank,))
                    bank_set((bank,), "V", fv[("stev", b)])
                    for kc in range(NDC):
                        ins = t.matmul(bs64, win_s[:, kc * M:(kc + 1) * M],
                                       af[:, kc * T + b * 512: kc * T + b * 512 + 512],
                                       start=(kc == 0), stop=(kc == NDC - 1))
                    ins.then_inc(sT, 1)
                elif key[0] == "HT":
                    m, b = key[1], key[2]
                    if m == 0:
                        t.wait_ge(sV, fv[("stev", b)])
                    bank_gate(t, (bank,))
                    bank_set((bank,), "VA"[0 if b < 2 else 1],
                             fv[("htev", (m, b))] if b < 2
                             else fa[("htev", (m, b))])
                    t.matmul(bs, w1e_s[:, m * 128:(m + 1) * 128],
                             sT_s[:, b * 512:(b + 1) * 512],
                             start=True, stop=True).then_inc(sT, 1)
                elif key[0] == "gT":
                    b = key[1]
                    if b < 2:
                        t.wait_ge(sV, fv[("htev", (7, b))])
                    else:
                        t.wait_ge(sA, fa[("htev", (7, b))])
                    bank_gate(t, (bank,))
                    bank_set((bank,), "V", fv[("gtev", b)])
                    for m in range(NDC):
                        ins = t.matmul(bs64, w2eT_s[:, m * M:(m + 1) * M],
                                       HT[:, m * T + b * 512: m * T + b * 512 + 512],
                                       start=(m == 0), stop=(m == NDC - 1))
                    ins.then_inc(sT, 1)
                else:
                    o = key[1]
                    tt16, vt = o // 2, o % 2
                    if vt == 0:
                        t.wait_ge(sV, fv[("gtev", tt16 // 4)])
                    if first_out:
                        t.wait_ge(sV, fv[("ones", 0)])
                        first_out = False
                    bank_gate(t, (bank,))
                    bank_set((bank,), "V", fv[("zpre", o // 2)])
                    t.matmul(bs, gT_s[0:M + 1, tt16 * 128:(tt16 + 1) * 128],
                             wout_s[0:M + 1, vt * 512:(vt + 1) * 512],
                             start=True, stop=True).then_inc(sT, 1)

        @block.vector
        def _(v):
            for kind, i in fv_seq:
                if kind == "ones":
                    v.memset(gT_s[M:M + 1, :], 1.0).then_inc(sV, 1)
                elif kind == "stev":
                    b = i
                    bank = bank_of[("sT", b)]
                    v.wait_ge(sT, T_ffn[("sT", b)])
                    v.tensor_scalar(sT_s[:, b * 512:(b + 1) * 512],
                                    ps[0:M, bank * 512:bank * 512 + 512],
                                    binc_s[:, 0:1], None,
                                    ALU.add).then_inc(sV, 1)
                elif kind == "htev":
                    m, b = i
                    bank = bank_of[("HT", m, b)]
                    v.wait_ge(sT, T_ffn[("HT", m, b)])
                    v.tensor_scalar(HT[:, m * T + b * 512: m * T + b * 512 + 512],
                                    ps[:, bank * 512:(bank + 1) * 512],
                                    b1T_s[:, m:m + 1], 0.0,
                                    ALU.add, ALU.max).then_inc(sV, 1)
                elif kind == "gtev":
                    b = i
                    bank = bank_of[("gT", b)]
                    v.wait_ge(sT, T_ffn[("gT", b)])
                    v.tensor_scalar(gT_s[0:M, b * 512:(b + 1) * 512],
                                    ps[0:M, bank * 512:bank * 512 + 512],
                                    b2c_s[:, 0:1], None,
                                    ALU.add).then_inc(sV, 1)
                elif kind == "af32":
                    it = i
                    so = it * D
                    v.wait_ge(sA, ra[("abf", it)])
                    v.tensor_scalar(stream[:, so:so + D], stream[:, so:so + D],
                                    negmean_s[:, it:it + 1], rstd_s[:, it:it + 1],
                                    ALU.add, ALU.mult).then_inc(sV, 1)
                elif kind == "zpre":
                    tt16 = i
                    so = tt16 * D
                    bank = bank_of[("out", 2 * tt16)]
                    if tt16 % 2 == 0:
                        v.wait_ge(sV, fv[("af32", tt16)])
                    else:
                        v.wait_ge(sA, fa[("af32", tt16)])
                    v.wait_ge(sT, T_ffn[("out", 2 * tt16 + 1)])
                    v.scalar_tensor_tensor(
                        stream[:, so:so + D], stream[:, so:so + D],
                        1.0, ps[:, bank * 512:bank * 512 + 1024],
                        ALU.mult, ALU.add,
                        accum_out=zs0_s[:, tt16:tt16 + 1]).then_inc(sV, 1)
                elif kind == "negmean24":
                    c4 = slice(16 + 2 * i, 16 + 2 * i + 2)
                    v.wait_ge(sV, fv[("zpre", 2 * i + 1)])
                    v.tensor_scalar(negmean_s[:, c4], zs0_s[:, 2 * i:2 * i + 2],
                                    -1.0 / D, None, ALU.mult).then_inc(sV, 1)
                elif kind == "m2_24":
                    c4 = slice(16 + 2 * i, 16 + 2 * i + 2)
                    v.wait_ge(sV, fv[("negmean24", i)])
                    v.tensor_tensor(m2e_s[:, c4], negmean_s[:, c4],
                                    negmean_s[:, c4], ALU.mult).then_inc(sV, 1)
                elif kind == "vepsa24":
                    c4 = slice(16 + 2 * i, 16 + 2 * i + 2)
                    v.wait_ge(sV, fv[("m2_24", i)])
                    v.wait_ge(sA, fa[("square2", 2 * i + 1)])
                    v.scalar_tensor_tensor(veps_s[:, c4], sqs2_s[:, 2 * i:2 * i + 2],
                                           1.0 / D, m2e_s[:, c4],
                                           ALU.mult, ALU.subtract).then_inc(sV, 1)
                elif kind == "veps24":
                    c4 = slice(16 + 2 * i, 16 + 2 * i + 2)
                    v.wait_ge(sV, fv[("vepsa24", i)])
                    v.tensor_scalar(veps_s[:, c4], veps_s[:, c4], 1e-5, None,
                                    ALU.add).then_inc(sV, 1)
                elif kind == "recip24":
                    c4 = slice(16 + 2 * i, 16 + 2 * i + 2)
                    v.wait_ge(sA, fa[("sqrt24", i)])
                    v.reciprocal(rstd_s[:, c4], sq_s[:, c4]).then_inc(sV, 1)
                elif kind == "negmr24":
                    c4 = slice(16 + 2 * i, 16 + 2 * i + 2)
                    v.wait_ge(sV, fv[("recip24", i)])
                    v.tensor_tensor(negmr_s[:, c4], negmean_s[:, c4],
                                    rstd_s[:, c4], ALU.mult).then_inc(sV, 1)
                elif kind == "norm":
                    tt16 = i
                    so = tt16 * D
                    v.wait_ge(sV, fv[("negmr24", tt16 // 2)])
                    v.tensor_scalar(stream[:, so:so + D], stream[:, so:so + D],
                                    negmean_s[:, 16 + tt16:17 + tt16],
                                    rstd_s[:, 16 + tt16:17 + tt16],
                                    ALU.add, ALU.mult).then_inc(sV, 1)

        @block.scalar
        def _(a):
            for kind, i in fa_seq:
                if kind == "af32":
                    it = i
                    so = it * D
                    a.wait_ge(sA, ra[("abf", it)])
                    a.wait_ge(sV, rv[("negmr4", it // 4)])
                    a.activation(stream[:, so:so + D], stream[:, so:so + D],
                                 AF.Identity, bias=negmr_s[:, it:it + 1],
                                 scale=rstd_s[:, it:it + 1]).then_inc(sA, 1)
                elif kind == "htev":
                    m, b = i
                    bank = bank_of[("HT", m, b)]
                    a.wait_ge(sT, T_ffn[("HT", m, b)])
                    a.activation(HT[:, m * T + b * 512: m * T + b * 512 + 512],
                                 ps[:, bank * 512:(bank + 1) * 512], AF.Relu,
                                 bias=b1T_s[:, m:m + 1], scale=1.0).then_inc(sA, 1)
                elif kind == "square2":
                    tt16 = i
                    so = tt16 * D
                    a.wait_ge(sV, fv[("zpre", tt16)])
                    a.wait_ge(sA, fa[("square2", tt16 - 1)] if tt16 >= 1
                              else ra[("square", 15)])
                    a.activation(junk_s[:, :], stream[:, so:so + D], AF.Square,
                                 accum_out=sqs2_s[:, tt16:tt16 + 1]).then_inc(sA, 1)
                elif kind == "sqrt24":
                    p = i
                    a.wait_ge(sV, fv[("veps24", p)])
                    a.activation(sq_s[:, 16 + 2 * p:16 + 2 * p + 2],
                                 veps_s[:, 16 + 2 * p:16 + 2 * p + 2],
                                 AF.Sqrt).then_inc(sA, 1)
                elif kind == "norm":
                    tt16 = i
                    so = tt16 * D
                    a.wait_ge(sV, fv[("negmr24", tt16 // 2)])
                    a.activation(stream[:, so:so + D], stream[:, so:so + D],
                                 AF.Identity, bias=negmr_s[:, 16 + tt16:17 + tt16],
                                 scale=rstd_s[:, 16 + tt16:17 + tt16]
                                 ).then_inc(sA, 1)

        @block.sync
        def _(s):
            for tt16 in range(16):
                if tt16 % 2 == 0:
                    s.wait_ge(sV, fv[("norm", tt16)])
                else:
                    s.wait_ge(sA, fa[("norm", tt16)])
                s.dma_start(out=z_h[tt16 * 128:(tt16 + 1) * 128, :],
                            in_=stream[:, tt16 * D:(tt16 + 1) * D]).then_inc(sZ, 16)
            s.wait_ge(sZ, 16 * 16)

        ffn.close()
        af_scope.close()
        a_roll_scope.close()

    right.close()
    left.close()
    sems.close()
    return nc


def _host_layout(inputs):
    f32 = np.float32
    wq = np.ascontiguousarray(inputs["L0_attn_Wq"], dtype=f32)
    wk = np.ascontiguousarray(inputs["L0_attn_Wk"], dtype=f32)
    wv = np.ascontiguousarray(inputs["L0_attn_Wv"], dtype=f32)
    bq = np.asarray(inputs["L0_attn_bq"], dtype=f32)
    bk = np.asarray(inputs["L0_attn_bk"], dtype=f32)
    bv = np.asarray(inputs["L0_attn_bv"], dtype=f32)
    win = np.ascontiguousarray(inputs["L0_ffn_Win"], dtype=f32)
    binp = np.asarray(inputs["L0_ffn_bin_"], dtype=f32)
    w1 = np.asarray(inputs["L0_ffn_W1"], dtype=f32)
    b1 = np.asarray(inputs["L0_ffn_b1"], dtype=f32)
    w2 = np.asarray(inputs["L0_ffn_W2"], dtype=f32)
    b2 = np.asarray(inputs["L0_ffn_b2"], dtype=f32)
    wout = np.ascontiguousarray(inputs["L0_ffn_Wout"], dtype=f32)
    bout = np.asarray(inputs["L0_ffn_bout"], dtype=f32)

    w1e = np.zeros((M, MH), dtype=f32)
    w2eT = np.zeros((MH, M), dtype=f32)
    rows = np.arange(M)
    cols = (np.arange(M) * INNER)[:, None] + np.arange(INNER)[None, :]
    w1e[rows[:, None], cols] = w1
    w2eT[cols, rows[:, None]] = w2

    d = {
        "wq": wq, "wk": wk, "wv": wv,
        "bqT": np.ascontiguousarray(bq.reshape(8, 128).T),
        "bkT": np.ascontiguousarray(bk.reshape(8, 128).T),
        "bvb": np.ascontiguousarray(np.broadcast_to(bv[None, :], (128, D))),
        "win": win,
        "binc": np.ascontiguousarray(binp.reshape(M, 1)),
        "w1e": w1e,
        "b1T": np.ascontiguousarray(b1.reshape(MH).reshape(8, 128).T),
        "w2eT": w2eT,
        "b2c": np.ascontiguousarray(b2.reshape(M, 1)),
        "wout": wout,
        "boutr": np.ascontiguousarray(bout.reshape(1, D)),
        "mask": np.ascontiguousarray(
            np.where(np.arange(128)[None, :] >= np.arange(128)[:, None],
                     np.float32(0.0), np.float32(-1e30))),
    }
    return d


def kernel(**inputs):
    xfull = np.ascontiguousarray(np.asarray(inputs["L0_attn_minibatch"], dtype=np.float32))
    shared = _host_layout(inputs)
    nc = build_nc()
    in_maps = []
    for b in range(8):
        m = dict(shared)
        m["x"] = np.ascontiguousarray(xfull[b])
        m["xt"] = np.ascontiguousarray(xfull[b].T)
        in_maps.append(m)
    res = run_bass_kernel_spmd(nc, in_maps, core_ids=list(range(8)))
    return np.stack([np.asarray(r["z"], dtype=np.float32) for r in res.results], axis=0)


# version 37 (final)
# speedup vs baseline: 1.0105x; 1.0105x over previous
import numpy as np
import concourse.bass as bass
import concourse.mybir as mybir
from concourse.bass_utils import run_bass_kernel_spmd
from contextlib import ExitStack

F32 = mybir.dt.float32
BF16 = mybir.dt.bfloat16
AF = mybir.ActivationFunctionType
ALU = mybir.AluOpType
AX = mybir.AxisListType

T, D = 2048, 1024
NTT, NDC = 16, 8
M, INNER, MH = 64, 16, 1024
CW = [T - 128 * jc for jc in range(16)]
COFF = [0]
for w in CW:
    COFF.append(COFF[-1] + w)
PTOT = COFF[-1]


def _slices512(w):
    out = []
    off = 0
    while off < w:
        out.append((off, min(512, w - off)))
        off += 512
    return out


def build_nc(debug=False):
    nc = bass.Bass("TRN2", target_bir_lowering=False)

    x_h = nc.dram_tensor("x", [T, D], F32, kind="ExternalInput").ap()
    xt_h = nc.dram_tensor("xt", [D, T], F32, kind="ExternalInput").ap()
    wq_h = nc.dram_tensor("wq", [D, D], F32, kind="ExternalInput").ap()
    wk_h = nc.dram_tensor("wk", [D, D], F32, kind="ExternalInput").ap()
    wv_h = nc.dram_tensor("wv", [D, D], F32, kind="ExternalInput").ap()
    bqT_h = nc.dram_tensor("bqT", [128, 8], F32, kind="ExternalInput").ap()
    bkT_h = nc.dram_tensor("bkT", [128, 8], F32, kind="ExternalInput").ap()
    bvb_h = nc.dram_tensor("bvb", [128, D], F32, kind="ExternalInput").ap()
    win_h = nc.dram_tensor("win", [D, M], F32, kind="ExternalInput").ap()
    binc_h = nc.dram_tensor("binc", [M, 1], F32, kind="ExternalInput").ap()
    w1e_h = nc.dram_tensor("w1e", [M, MH], F32, kind="ExternalInput").ap()
    b1T_h = nc.dram_tensor("b1T", [128, 8], F32, kind="ExternalInput").ap()
    w2eT_h = nc.dram_tensor("w2eT", [MH, M], F32, kind="ExternalInput").ap()
    b2c_h = nc.dram_tensor("b2c", [M, 1], F32, kind="ExternalInput").ap()
    wout_h = nc.dram_tensor("wout", [M, D], F32, kind="ExternalInput").ap()
    boutr_h = nc.dram_tensor("boutr", [1, D], F32, kind="ExternalInput").ap()
    mask_h = nc.dram_tensor("mask", [128, 128], F32, kind="ExternalInput").ap()
    z_h = nc.dram_tensor("z", [T, D], F32, kind="ExternalOutput").ap()
    abf_h = nc.dram_tensor("abf", [T, D], BF16, kind="Internal").ap()

    left = ExitStack()
    right = ExitStack()

    def sb(st, name, shape, dtype, side):
        return st.enter_context(nc.sbuf_tensor(name, shape, dtype, side=side))

    bvb_s = sb(left, "bvb_s", [128, D], BF16, "left")
    mask_s = sb(left, "mask_s", [128, 128], F32, "left")
    bqT_s = sb(left, "bqT_s", [128, 8], F32, "left")
    bkT_s = sb(left, "bkT_s", [128, 8], F32, "left")
    S_s = sb(left, "S_s", [128, 16], F32, "left")
    negmax_s = sb(left, "negmax_s", [128, 16], F32, "left")
    R_s = sb(left, "R_s", [128, 16], F32, "left")
    ys_s = sb(left, "ys_s", [128, 16], F32, "left")
    sqs_s = sb(left, "sqs_s", [128, 16], F32, "left")
    zs0_s = sb(left, "zs0_s", [128, 16], F32, "left")
    zs1_s = sb(left, "zs1_s", [128, 16], F32, "left")
    sqs2_s = sb(left, "sqs2_s", [128, 16], F32, "left")
    negmean_s = sb(left, "negmean_s", [128, 32], F32, "left")
    m2e_s = sb(left, "m2e_s", [128, 32], F32, "left")
    veps_s = sb(left, "veps_s", [128, 32], F32, "left")
    sq_s = sb(left, "sq_s", [128, 32], F32, "left")
    rstd_s = sb(left, "rstd_s", [128, 32], F32, "left")
    negmr_s = sb(left, "negmr_s", [128, 32], F32, "left")
    junk_s = sb(left, "junk_s", [128, 1024], BF16, "left")
    win_s = sb(left, "win_s", [128, NDC * M], BF16, "left")
    w1e_s = sb(left, "w1e_s", [M, MH], BF16, "left")
    w2eT_s = sb(left, "w2eT_s", [128, NDC * M], BF16, "left")
    wout_s = sb(left, "wout_s", [M + 1, D], BF16, "left")
    b1T_s = sb(left, "b1T_s", [128, 8], F32, "left")
    binc_s = sb(left, "binc_s", [M, 1], F32, "left")
    b2c_s = sb(left, "b2c_s", [M, 1], F32, "left")
    ps = left.enter_context(nc.psum_tensor("ps", [128, 4096], F32))

    vp_scope = ExitStack()
    V_bf = vp_scope.enter_context(nc.sbuf_tensor("V_bf", [128, NTT * D], BF16, side="left"))
    probsT = vp_scope.enter_context(nc.sbuf_tensor("probsT", [128, PTOT], BF16, side="left"))
    qk_scope = ExitStack()
    Qf = qk_scope.enter_context(nc.sbuf_tensor("Qf", [128, NDC * T], BF16, side="left"))
    Kf = qk_scope.enter_context(nc.sbuf_tensor("Kf", [128, NDC * T], BF16, side="left"))
    r1_scope = ExitStack()
    xf = r1_scope.enter_context(nc.sbuf_tensor("xf", [128, NDC * T], BF16, side="right"))
    wb0 = r1_scope.enter_context(nc.sbuf_tensor("wb0", [128, NDC * D], BF16, side="right"))
    wb1 = r1_scope.enter_context(nc.sbuf_tensor("wb1", [128, NDC * D], BF16, side="right"))

    sems = ExitStack()

    def sem(name):
        return sems.enter_context(nc.semaphore(name))

    sWqC = [sem(f"sWqC{c}") for c in range(NDC)]
    sWk, sWv, sBv = sem("sWk"), sem("sWv"), sem("sBv")
    sLsm, sLf, sL1 = sem("sLsm"), sem("sLf"), sem("sL1")
    sTP, sSx, sSt, sZ = sem("sTP"), sem("sSx"), sem("sSt"), sem("sZ")
    sXf = [sem(f"sXf{k}") for k in range(4)]
    sT, sV, sA = sem("sT"), sem("sV"), sem("sA")

    T_tile = lambda q: q + 1
    T_QDONE = 32
    T_QKV = 96
    T_chunk = lambda pos: 96 + pos + 1
    T_read = lambda it: 112 + it + 1
    T_READ = 128
    ffn_tiles = []
    for b in range(4):
        ffn_tiles.append(("sT", b))
        ffn_tiles += [("HT", m, b) for m in range(NDC)]
        ffn_tiles.append(("gT", b))
        ffn_tiles += [("out", 8 * b + i) for i in range(8)]
    T_ffn = {}
    bank_of = {}
    for i, key in enumerate(ffn_tiles):
        T_ffn[key] = T_READ + i + 1
        if key[0] in ("sT", "gT"):
            bank_of[key] = 6 + key[1] % 2
        elif key[0] == "HT":
            bank_of[key] = 2 + key[1] % 4
        else:
            bank_of[key] = (0, 1, 2, 3, 4, 5, 0, 1)[key[1] % 8]

    V_evac = lambda q: q + 1
    V_QKV = 96
    A_exp = lambda jc: jc + 1
    A_EXP = 16

    def _ln_order(ops_young, ops_mid, ops_old, ngrp=4):
        seq = []
        for s in range(ngrp + 2):
            if 1 <= s <= ngrp:
                seq += ops_mid(s - 1)
            if 2 <= s <= ngrp + 1:
                seq += ops_old(s - 2)
            if s < ngrp:
                seq += ops_young(s)
        return seq

    _score5 = lambda jc: [("mask", jc), ("max", jc), ("Rt", jc),
                          ("screcip", jc), ("scale", jc)]
    JCO_POS = {jc: pos for pos, jc in enumerate([12, 13, 14, 15] + list(range(12)))}
    rv_seq = _ln_order(
        lambda g: [("stt", 4 * g + i) for i in range(4) if 4 * g + i >= 2]
        + [("negmean4", g), ("m2_4", g)],
        lambda g: [("vepsa4", g), ("veps4", g)],
        lambda g: [("recip4", g), ("negmr4", g)])
    JCO_SCHED = [12, 13, 14, 15] + list(range(12))
    sv_seq = (sum((_score5(jc) for jc in JCO_SCHED[:15]), [])
              + [("stt", 0), ("stt", 1)]
              + _score5(JCO_SCHED[15])
              + rv_seq)
    rv = {}
    vc = V_QKV
    for k in sv_seq:
        vc += 1
        rv[k] = vc
    V_READ = vc
    V_scale = lambda jc: rv[("scale", jc)]
    ra_seq = _ln_order(
        lambda g: [("square", 4 * g + i) for i in range(4)],
        lambda g: [("sqrt4", g)],
        lambda g: [("abf", 4 * g + i) for i in range(4)])
    ra = {}
    ac = A_EXP
    for k in ra_seq:
        ac += 1
        ra[k] = ac
    A_READ = ac

    def _ffn_stages(stages, ngrp=4):
        seq = []
        for s in range(ngrp + len(stages) - 1):
            for lag, ops in enumerate(reversed(stages)):
                b = s - (len(stages) - 1 - lag)
                if 0 <= b < ngrp:
                    seq += ops(b)
        return seq

    fv_seq = [("ones", 0)] + _ffn_stages([
        lambda b: [("stev", b)] + ([("htev", (m, b)) for m in range(NDC)]
                                   if b < 2 else []),
        lambda b: [("af32", 4 * b + i) for i in range(4) if i % 2 == 0]
        + [("gtev", b)] + [("zpre", 4 * b + i) for i in range(4)]
        + [("negmean24", 2 * b), ("m2_24", 2 * b),
           ("negmean24", 2 * b + 1), ("m2_24", 2 * b + 1)],
        lambda b: [("vepsa24", 2 * b), ("veps24", 2 * b),
                   ("vepsa24", 2 * b + 1), ("veps24", 2 * b + 1)],
        lambda b: [("recip24", 2 * b), ("negmr24", 2 * b), ("norm", 4 * b),
                   ("recip24", 2 * b + 1), ("negmr24", 2 * b + 1),
                   ("norm", 4 * b + 2)]])
    fv = {}
    vc = V_READ
    for k in fv_seq:
        vc += 1
        fv[k] = vc
    V_FFN = vc
    fa_seq = _ffn_stages([
        lambda b: [("htev", (m, b)) for m in range(NDC)] if b >= 2 else [],
        lambda b: [("af32", 4 * b + i) for i in range(4) if i % 2 == 1]
        + [("square2", 4 * b + i) for i in range(4)],
        lambda b: [("sqrt24", 2 * b), ("sqrt24", 2 * b + 1)],
        lambda b: [("norm", 4 * b + 1), ("norm", 4 * b + 3)]])
    fa = {}
    ac = A_READ
    for k in fa_seq:
        ac += 1
        fa[k] = ac
    A_FFN = ac

    qkv_tiles = ([("Q", oc, nt) for nt in range(4) for oc in range(NDC)]
                 + [("K", oc, nt) for nt in range(4) for oc in range(NDC)]
                 + [("V", tt, vt) for tt in range(NTT) for vt in range(2)])

    with nc.Block() as block:
        xf3 = xf[:, :].rearrange("p (c t) -> p c t", c=NDC)
        xt3 = xt_h.rearrange("(c p) t -> p c t", p=128)
        wq3 = wq_h.rearrange("(c p) o -> p c o", p=128)
        wb03 = wb0[:, :].rearrange("p (c o) -> p c o", c=NDC)

        @block.gpsimd
        def _(g):
            g.dma_start(out=xf3[:, :, 0:512], in_=xt3[:, :, 0:512]).then_inc(sXf[0], 16)
            for c in range(NDC):
                g.dma_start(out=wb03[:, c:c + 1, :],
                            in_=wq3[:, c:c + 1, :]).then_inc(sWqC[c], 16)
            g.dma_start(out=xf3[:, :, 512:1024],
                        in_=xt3[:, :, 512:1024]).then_inc(sXf[1], 16)
            g.dma_start(out=xf3[:, :, 1024:1536],
                        in_=xt3[:, :, 1024:1536]).then_inc(sXf[2], 16)
            g.dma_start(out=xf3[:, :, 1536:2048],
                        in_=xt3[:, :, 1536:2048]).then_inc(sXf[3], 16)
            g.dma_start(out=wb1[:, :].rearrange("p (c o) -> p c o", c=NDC),
                        in_=wk_h.rearrange("(c p) o -> p c o", p=128)).then_inc(sWk, 16)
            g.dma_start(out=bvb_s[:, :], in_=bvb_h).then_inc(sBv, 16)
            g.wait_ge(sT, T_QDONE)
            g.dma_start(out=wb03,
                        in_=wv_h.rearrange("(c p) o -> p c o", p=128)).then_inc(sWv, 16)
            g.dma_start(out=win_s[:, :].rearrange("p (c m) -> p c m", c=NDC),
                        in_=win_h.rearrange("(c p) m -> p c m", p=128)).then_inc(sLf, 16)
            g.dma_start(out=w1e_s[:, :], in_=w1e_h).then_inc(sLf, 16)
            g.dma_start(out=w2eT_s[:, :].rearrange("p (c m) -> p c m", c=NDC),
                        in_=w2eT_h.rearrange("(c p) m -> p c m", p=128)).then_inc(sLf, 16)
            g.dma_start(out=wout_s[0:M, :], in_=wout_h).then_inc(sLf, 16)
            g.dma_start(out=b1T_s[:, :], in_=b1T_h).then_inc(sLf, 16)
            g.dma_start(out=binc_s[:, :], in_=binc_h).then_inc(sLf, 16)
            g.dma_start(out=b2c_s[:, :], in_=b2c_h).then_inc(sLf, 16)
            g.dma_start(out=wout_s[M:M + 1, :], in_=boutr_h).then_inc(sL1, 16)

        @block.sync
        def _(s):
            s.dma_start(out=bqT_s[:, :], in_=bqT_h).then_inc(sLsm, 16)
            s.dma_start(out=bkT_s[:, :], in_=bkT_h).then_inc(sLsm, 16)
            s.dma_start(out=mask_s[:, :], in_=mask_h).then_inc(sLsm, 16)

        @block.tensor
        def _(t):
            t.wait_ge(sXf[0], 16)
            for half in range(2):
                for kc in range(4 * half, 4 * half + 4):
                    t.wait_ge(sWqC[kc], 16)
                    for q in range(8):
                        ins = t.matmul(
                            ps[:, q * 512:(q + 1) * 512],
                            wb0[:, kc * D + q * 128: kc * D + q * 128 + 128],
                            xf[:, kc * T: kc * T + 512],
                            start=(kc == 0), stop=(kc == NDC - 1))
                        if kc == NDC - 1:
                            ins.then_inc(sT, 1)
            for q, (proj, a0, a1) in enumerate(qkv_tiles):
                if q < 8:
                    continue
                if proj == "Q" and a0 == 0:
                    t.wait_ge(sXf[a1], 16)
                if proj == "K" and q == 32:
                    t.wait_ge(sWk, 16)
                if proj == "V" and q == 64:
                    t.wait_ge(sWv, 16)
                wb = {"Q": wb0, "K": wb1, "V": wb0}[proj]
                bank = q % 8
                if q >= 8:
                    t.wait_ge(sV, V_evac(q - 8))
                for kc in range(NDC):
                    if proj != "V":
                        lhsT = wb[:, kc * D + a0 * 128: kc * D + a0 * 128 + 128]
                        rhs = xf[:, kc * T + a1 * 512: kc * T + a1 * 512 + 512]
                    else:
                        lhsT = xf[:, kc * T + a0 * 128: kc * T + a0 * 128 + 128]
                        rhs = wb[:, kc * D + a1 * 512: kc * D + a1 * 512 + 512]
                    ins = t.matmul(ps[:, bank * 512:(bank + 1) * 512],
                                   lhsT, rhs, start=(kc == 0), stop=(kc == NDC - 1))
                ins.then_inc(sT, 1)

        @block.vector
        def _(v):
            v.wait_ge(sLsm, 48)
            for q, (proj, a0, a1) in enumerate(qkv_tiles):
                bank = q % 8
                pslice = ps[:, bank * 512:(bank + 1) * 512]
                if proj == "V" and q == 64:
                    v.wait_ge(sBv, 16)
                v.wait_ge(sT, T_tile(q))
                if proj == "Q":
                    v.tensor_scalar(Qf[:, a0 * T + a1 * 512: a0 * T + a1 * 512 + 512],
                                    pslice, bqT_s[:, a0:a0 + 1], None,
                                    ALU.add).then_inc(sV, 1)
                elif proj == "K":
                    v.tensor_scalar(Kf[:, a0 * T + a1 * 512: a0 * T + a1 * 512 + 512],
                                    pslice, bkT_s[:, a0:a0 + 1], None,
                                    ALU.add).then_inc(sV, 1)
                else:
                    v.tensor_tensor(V_bf[:, a0 * D + a1 * 512: a0 * D + a1 * 512 + 512],
                                    pslice, bvb_s[:, a1 * 512:(a1 + 1) * 512],
                                    ALU.add).then_inc(sV, 1)

        r1_scope.close()

        JCO = [12, 13, 14, 15] + list(range(12))
        SPLACE = {12: (4, 1), 13: (5, 1), 14: (6, 1), 15: (7, 1),
                  0: (0, 4), 1: (4, 4), 2: (0, 4), 3: (4, 4),
                  4: (0, 3), 5: (4, 3), 6: (0, 3), 7: (4, 3),
                  8: (0, 2), 9: (4, 2), 10: (2, 2), 11: (6, 2)}
        A_expd = {jc: pos + 1 for pos, jc in enumerate(JCO)}
        bank_owner = {b: None for b in range(8)}

        def bank_gate(t, banks):
            g = {}
            for b in banks:
                o = bank_owner[b]
                if o:
                    k, val = o
                    g[k] = max(g.get(k, 0), val)
            for k, val in sorted(g.items()):
                t.wait_ge({"V": sV, "A": sA}[k], val)

        def bank_set(banks, kind, val):
            for b in banks:
                bank_owner[b] = (kind, val)

        @block.tensor
        def _(t):
            t.wait_ge(sV, V_QKV)
            for pos, jc in enumerate(JCO):
                W = CW[jc]
                b0, nb = SPLACE[jc]
                boff = b0 * 512
                bank_gate(t, range(b0, b0 + nb))
                bank_set(range(b0, b0 + nb), "A", A_expd[jc])
                for (soff, sw) in _slices512(W):
                    for kc in range(NDC):
                        ins = t.matmul(
                            ps[:, boff + soff: boff + soff + sw],
                            Kf[:, kc * T + jc * 128: kc * T + jc * 128 + 128],
                            Qf[:, kc * T + jc * 128 + soff: kc * T + jc * 128 + soff + sw],
                            start=(kc == 0), stop=(kc == NDC - 1))
                ins.then_inc(sT, 1)

        @block.scalar
        def _(a):
            for jc in JCO:
                W = CW[jc]
                boff = SPLACE[jc][0] * 512
                a.wait_ge(sV, rv[("max", jc)])
                a.activation(probsT[:, COFF[jc]:COFF[jc] + W], ps[:, boff:boff + W],
                             AF.Exp, bias=negmax_s[:, jc:jc + 1], scale=1.0,
                             accum_out=S_s[:, jc:jc + 1]).then_inc(sA, 1)

        qk_scope.close()
        stream = right.enter_context(nc.sbuf_tensor("stream", [128, NTT * D], F32, side="right"))
        a_roll_scope = ExitStack()
        a_roll = a_roll_scope.enter_context(
            nc.sbuf_tensor("a_roll", [128, 4 * D], BF16, side="right"))

        @block.sync
        def _(s):
            s.wait_ge(sT, T_QKV)
            s.dma_start(out=stream[:, :].rearrange("p (tt d) -> p tt d", tt=NTT),
                        in_=x_h.rearrange("(tt p) d -> p tt d", p=128)).then_inc(sSx, 16)

        @block.tensor
        def _(t):
            for it in range(16):
                boff = (it % 2) * 1024
                banks = (2 * (it % 2), 2 * (it % 2) + 1)
                t.wait_ge(sV, V_scale(it))
                bank_gate(t, banks)
                bank_set(banks, "V", rv[("stt", it)])
                for vt in range(2):
                    for jc in range(it + 1):
                        ins = t.matmul(
                            ps[:, boff + vt * 512: boff + vt * 512 + 512],
                            probsT[:, COFF[jc] + (it - jc) * 128: COFF[jc] + (it - jc) * 128 + 128],
                            V_bf[:, jc * D + vt * 512: jc * D + vt * 512 + 512],
                            start=(jc == 0), stop=(jc == it))
                ins.then_inc(sT, 1)

        @block.vector
        def _(v):
            sSx_waited = [False]
            for kind, i in sv_seq:
                if kind == "mask":
                    jc = i
                    boff = SPLACE[jc][0] * 512
                    v.wait_ge(sT, T_chunk(JCO_POS[jc]))
                    v.tensor_tensor(ps[:, boff:boff + 128], ps[:, boff:boff + 128],
                                    mask_s[:, :], ALU.add).then_inc(sV, 1)
                elif kind == "max":
                    jc = i
                    boff = SPLACE[jc][0] * 512
                    v.wait_ge(sV, rv[("mask", jc)])
                    v.tensor_reduce(negmax_s[:, jc:jc + 1], ps[:, boff:boff + CW[jc]],
                                    AX.X, ALU.max, negate=True).then_inc(sV, 1)
                elif kind == "Rt":
                    jc = i
                    v.wait_ge(sA, A_expd[jc])
                    v.tensor_scalar(R_s[:, jc:jc + 1], S_s[:, jc:jc + 1], 32.0, None,
                                    ALU.mult).then_inc(sV, 1)
                elif kind == "screcip":
                    jc = i
                    v.wait_ge(sV, rv[("Rt", jc)])
                    v.reciprocal(R_s[:, jc:jc + 1], R_s[:, jc:jc + 1]).then_inc(sV, 1)
                elif kind == "scale":
                    jc = i
                    v.wait_ge(sV, rv[("screcip", jc)])
                    v.tensor_scalar(probsT[:, COFF[jc]:COFF[jc] + CW[jc]],
                                    probsT[:, COFF[jc]:COFF[jc] + CW[jc]],
                                    R_s[:, jc:jc + 1], None, ALU.mult).then_inc(sV, 1)
                elif kind == "stt":
                    it = i
                    if not sSx_waited[0]:
                        v.wait_ge(sSx, 16)
                        sSx_waited[0] = True
                    boff = (it % 2) * 1024
                    so = it * D
                    v.wait_ge(sT, T_read(it))
                    v.scalar_tensor_tensor(
                        stream[:, so:so + D], stream[:, so:so + D],
                        2.0, ps[:, boff:boff + 1024], ALU.mult, ALU.add,
                        accum_out=ys_s[:, it:it + 1]).then_inc(sV, 1)
                elif kind == "negmean4":
                    c4 = slice(4 * i, 4 * i + 4)
                    v.wait_ge(sV, rv[("stt", 4 * i + 3)])
                    v.tensor_scalar(negmean_s[:, c4], ys_s[:, c4],
                                    -1.0 / D, None, ALU.mult).then_inc(sV, 1)
                elif kind == "m2_4":
                    c4 = slice(4 * i, 4 * i + 4)
                    v.wait_ge(sV, rv[("negmean4", i)])
                    v.tensor_tensor(m2e_s[:, c4], negmean_s[:, c4],
                                    negmean_s[:, c4], ALU.mult).then_inc(sV, 1)
                elif kind == "vepsa4":
                    c4 = slice(4 * i, 4 * i + 4)
                    v.wait_ge(sV, rv[("m2_4", i)])
                    v.wait_ge(sA, ra[("square", 4 * i + 3)])
                    v.scalar_tensor_tensor(veps_s[:, c4], sqs_s[:, c4],
                                           1.0 / D, m2e_s[:, c4],
                                           ALU.mult, ALU.subtract).then_inc(sV, 1)
                elif kind == "veps4":
                    c4 = slice(4 * i, 4 * i + 4)
                    v.wait_ge(sV, rv[("vepsa4", i)])
                    v.tensor_scalar(veps_s[:, c4], veps_s[:, c4], 1e-5, None,
                                    ALU.add).then_inc(sV, 1)
                elif kind == "recip4":
                    c4 = slice(4 * i, 4 * i + 4)
                    v.wait_ge(sA, ra[("sqrt4", i)])
                    v.reciprocal(rstd_s[:, c4], sq_s[:, c4]).then_inc(sV, 1)
                elif kind == "negmr4":
                    c4 = slice(4 * i, 4 * i + 4)
                    v.wait_ge(sV, rv[("recip4", i)])
                    v.tensor_tensor(negmr_s[:, c4], negmean_s[:, c4],
                                    rstd_s[:, c4], ALU.mult).then_inc(sV, 1)

        @block.scalar
        def _(a):
            for kind, i in ra_seq:
                if kind == "square":
                    it = i
                    so = it * D
                    a.wait_ge(sV, rv[("stt", it)])
                    if it >= 1:
                        a.wait_ge(sA, ra[("square", it - 1)])
                    a.activation(junk_s[:, :], stream[:, so:so + D], AF.Square,
                                 accum_out=sqs_s[:, it:it + 1]).then_inc(sA, 1)
                elif kind == "sqrt4":
                    g = i
                    a.wait_ge(sV, rv[("veps4", g)])
                    a.activation(sq_s[:, 4 * g:4 * g + 4], veps_s[:, 4 * g:4 * g + 4],
                                 AF.Sqrt).then_inc(sA, 1)
                elif kind == "abf":
                    it = i
                    so = it * D
                    slot = (it % 4) * D
                    if it >= 4:
                        a.wait_ge(sTP, 16 * (it // 4))
                    a.wait_ge(sV, rv[("negmr4", it // 4)])
                    a.activation(a_roll[:, slot:slot + D], stream[:, so:so + D],
                                 AF.Identity, bias=negmr_s[:, it:it + 1],
                                 scale=rstd_s[:, it:it + 1]).then_inc(sA, 1)

        af_scope = ExitStack()
        af = af_scope.enter_context(nc.sbuf_tensor("af", [128, NDC * T], BF16, side="right"))
        af3 = af[:, :].rearrange("p (c t) -> p c t", c=NDC)

        @block.sync
        def _(s):
            s.wait_ge(sWv, 16)
            s.wait_ge(sLf, 16 * 7)
            s.wait_ge(sL1, 16)
            s.wait_ge(sSx, 16)
            for g in range(4):
                for it in range(4 * g, 4 * g + 4):
                    s.wait_ge(sA, ra[("abf", it)])
                    s.dma_start(out=abf_h[it * 128:(it + 1) * 128, :],
                                in_=a_roll[:, (it % 4) * D:(it % 4) * D + D]
                                ).then_inc(sSt, 16)
                s.wait_ge(sSt, 16 * (4 * g + 4))
                s.dma_start(out=af3[:, :, 512 * g:512 * (g + 1)],
                            in_=abf_h[512 * g:512 * (g + 1), :],
                            transpose=True).then_inc(sTP, 16)

        vp_scope.close()
        ffn = ExitStack()
        HT = ffn.enter_context(nc.sbuf_tensor("HT", [128, NDC * T], BF16, side="left"))
        sT_s = ffn.enter_context(nc.sbuf_tensor("sT_s", [M, T], BF16, side="left"))
        gT_s = ffn.enter_context(nc.sbuf_tensor("gT_s", [M + 1, T], BF16, side="left"))

        @block.tensor
        def _(t):
            first_out = True
            for key in ffn_tiles:
                bank = bank_of[key]
                bs = ps[:, bank * 512:(bank + 1) * 512]
                bs64 = ps[0:M, bank * 512:bank * 512 + 512]
                if key[0] == "sT":
                    b = key[1]
                    t.wait_ge(sTP, 16 * (b + 1))
                    bank_gate(t, (bank,))
                    bank_set((bank,), "V", fv[("stev", b)])
                    for kc in range(NDC):
                        ins = t.matmul(bs64, win_s[:, kc * M:(kc + 1) * M],
                                       af[:, kc * T + b * 512: kc * T + b * 512 + 512],
                                       start=(kc == 0), stop=(kc == NDC - 1))
                    ins.then_inc(sT, 1)
                elif key[0] == "HT":
                    m, b = key[1], key[2]
                    if m == 0:
                        t.wait_ge(sV, fv[("stev", b)])
                    bank_gate(t, (bank,))
                    bank_set((bank,), "VA"[0 if b < 2 else 1],
                             fv[("htev", (m, b))] if b < 2
                             else fa[("htev", (m, b))])
                    t.matmul(bs, w1e_s[:, m * 128:(m + 1) * 128],
                             sT_s[:, b * 512:(b + 1) * 512],
                             start=True, stop=True).then_inc(sT, 1)
                elif key[0] == "gT":
                    b = key[1]
                    if b < 2:
                        t.wait_ge(sV, fv[("htev", (7, b))])
                    else:
                        t.wait_ge(sA, fa[("htev", (7, b))])
                    bank_gate(t, (bank,))
                    bank_set((bank,), "V", fv[("gtev", b)])
                    for m in range(NDC):
                        ins = t.matmul(bs64, w2eT_s[:, m * M:(m + 1) * M],
                                       HT[:, m * T + b * 512: m * T + b * 512 + 512],
                                       start=(m == 0), stop=(m == NDC - 1))
                    ins.then_inc(sT, 1)
                else:
                    o = key[1]
                    tt16, vt = o // 2, o % 2
                    if vt == 0:
                        t.wait_ge(sV, fv[("gtev", tt16 // 4)])
                    if first_out:
                        t.wait_ge(sV, fv[("ones", 0)])
                        first_out = False
                    bank_gate(t, (bank,))
                    bank_set((bank,), "V", fv[("zpre", o // 2)])
                    t.matmul(bs, gT_s[0:M + 1, tt16 * 128:(tt16 + 1) * 128],
                             wout_s[0:M + 1, vt * 512:(vt + 1) * 512],
                             start=True, stop=True).then_inc(sT, 1)

        @block.vector
        def _(v):
            for kind, i in fv_seq:
                if kind == "ones":
                    v.memset(gT_s[M:M + 1, :], 1.0).then_inc(sV, 1)
                elif kind == "stev":
                    b = i
                    bank = bank_of[("sT", b)]
                    v.wait_ge(sT, T_ffn[("sT", b)])
                    v.tensor_scalar(sT_s[:, b * 512:(b + 1) * 512],
                                    ps[0:M, bank * 512:bank * 512 + 512],
                                    binc_s[:, 0:1], None,
                                    ALU.add).then_inc(sV, 1)
                elif kind == "htev":
                    m, b = i
                    bank = bank_of[("HT", m, b)]
                    v.wait_ge(sT, T_ffn[("HT", m, b)])
                    v.tensor_scalar(HT[:, m * T + b * 512: m * T + b * 512 + 512],
                                    ps[:, bank * 512:(bank + 1) * 512],
                                    b1T_s[:, m:m + 1], 0.0,
                                    ALU.add, ALU.max).then_inc(sV, 1)
                elif kind == "gtev":
                    b = i
                    bank = bank_of[("gT", b)]
                    v.wait_ge(sT, T_ffn[("gT", b)])
                    v.tensor_scalar(gT_s[0:M, b * 512:(b + 1) * 512],
                                    ps[0:M, bank * 512:bank * 512 + 512],
                                    b2c_s[:, 0:1], None,
                                    ALU.add).then_inc(sV, 1)
                elif kind == "af32":
                    it = i
                    so = it * D
                    v.wait_ge(sA, ra[("abf", it)])
                    v.tensor_scalar(stream[:, so:so + D], stream[:, so:so + D],
                                    negmean_s[:, it:it + 1], rstd_s[:, it:it + 1],
                                    ALU.add, ALU.mult).then_inc(sV, 1)
                elif kind == "zpre":
                    tt16 = i
                    so = tt16 * D
                    bank = bank_of[("out", 2 * tt16)]
                    if tt16 % 2 == 0:
                        v.wait_ge(sV, fv[("af32", tt16)])
                    else:
                        v.wait_ge(sA, fa[("af32", tt16)])
                    v.wait_ge(sT, T_ffn[("out", 2 * tt16 + 1)])
                    v.scalar_tensor_tensor(
                        stream[:, so:so + D], stream[:, so:so + D],
                        1.0, ps[:, bank * 512:bank * 512 + 1024],
                        ALU.mult, ALU.add,
                        accum_out=zs0_s[:, tt16:tt16 + 1]).then_inc(sV, 1)
                elif kind == "negmean24":
                    c4 = slice(16 + 2 * i, 16 + 2 * i + 2)
                    v.wait_ge(sV, fv[("zpre", 2 * i + 1)])
                    v.tensor_scalar(negmean_s[:, c4], zs0_s[:, 2 * i:2 * i + 2],
                                    -1.0 / D, None, ALU.mult).then_inc(sV, 1)
                elif kind == "m2_24":
                    c4 = slice(16 + 2 * i, 16 + 2 * i + 2)
                    v.wait_ge(sV, fv[("negmean24", i)])
                    v.tensor_tensor(m2e_s[:, c4], negmean_s[:, c4],
                                    negmean_s[:, c4], ALU.mult).then_inc(sV, 1)
                elif kind == "vepsa24":
                    c4 = slice(16 + 2 * i, 16 + 2 * i + 2)
                    v.wait_ge(sV, fv[("m2_24", i)])
                    v.wait_ge(sA, fa[("square2", 2 * i + 1)])
                    v.scalar_tensor_tensor(veps_s[:, c4], sqs2_s[:, 2 * i:2 * i + 2],
                                           1.0 / D, m2e_s[:, c4],
                                           ALU.mult, ALU.subtract).then_inc(sV, 1)
                elif kind == "veps24":
                    c4 = slice(16 + 2 * i, 16 + 2 * i + 2)
                    v.wait_ge(sV, fv[("vepsa24", i)])
                    v.tensor_scalar(veps_s[:, c4], veps_s[:, c4], 1e-5, None,
                                    ALU.add).then_inc(sV, 1)
                elif kind == "recip24":
                    c4 = slice(16 + 2 * i, 16 + 2 * i + 2)
                    v.wait_ge(sA, fa[("sqrt24", i)])
                    v.reciprocal(rstd_s[:, c4], sq_s[:, c4]).then_inc(sV, 1)
                elif kind == "negmr24":
                    c4 = slice(16 + 2 * i, 16 + 2 * i + 2)
                    v.wait_ge(sV, fv[("recip24", i)])
                    v.tensor_tensor(negmr_s[:, c4], negmean_s[:, c4],
                                    rstd_s[:, c4], ALU.mult).then_inc(sV, 1)
                elif kind == "norm":
                    tt16 = i
                    so = tt16 * D
                    v.wait_ge(sV, fv[("negmr24", tt16 // 2)])
                    v.tensor_scalar(stream[:, so:so + D], stream[:, so:so + D],
                                    negmean_s[:, 16 + tt16:17 + tt16],
                                    rstd_s[:, 16 + tt16:17 + tt16],
                                    ALU.add, ALU.mult).then_inc(sV, 1)

        @block.scalar
        def _(a):
            for kind, i in fa_seq:
                if kind == "af32":
                    it = i
                    so = it * D
                    a.wait_ge(sA, ra[("abf", it)])
                    a.wait_ge(sV, rv[("negmr4", it // 4)])
                    a.activation(stream[:, so:so + D], stream[:, so:so + D],
                                 AF.Identity, bias=negmr_s[:, it:it + 1],
                                 scale=rstd_s[:, it:it + 1]).then_inc(sA, 1)
                elif kind == "htev":
                    m, b = i
                    bank = bank_of[("HT", m, b)]
                    a.wait_ge(sT, T_ffn[("HT", m, b)])
                    a.activation(HT[:, m * T + b * 512: m * T + b * 512 + 512],
                                 ps[:, bank * 512:(bank + 1) * 512], AF.Relu,
                                 bias=b1T_s[:, m:m + 1], scale=1.0).then_inc(sA, 1)
                elif kind == "square2":
                    tt16 = i
                    so = tt16 * D
                    a.wait_ge(sV, fv[("zpre", tt16)])
                    a.wait_ge(sA, fa[("square2", tt16 - 1)] if tt16 >= 1
                              else ra[("square", 15)])
                    a.activation(junk_s[:, :], stream[:, so:so + D], AF.Square,
                                 accum_out=sqs2_s[:, tt16:tt16 + 1]).then_inc(sA, 1)
                elif kind == "sqrt24":
                    p = i
                    a.wait_ge(sV, fv[("veps24", p)])
                    a.activation(sq_s[:, 16 + 2 * p:16 + 2 * p + 2],
                                 veps_s[:, 16 + 2 * p:16 + 2 * p + 2],
                                 AF.Sqrt).then_inc(sA, 1)
                elif kind == "norm":
                    tt16 = i
                    so = tt16 * D
                    a.wait_ge(sV, fv[("negmr24", tt16 // 2)])
                    a.activation(stream[:, so:so + D], stream[:, so:so + D],
                                 AF.Identity, bias=negmr_s[:, 16 + tt16:17 + tt16],
                                 scale=rstd_s[:, 16 + tt16:17 + tt16]
                                 ).then_inc(sA, 1)

        @block.sync
        def _(s):
            for tt16 in range(16):
                if tt16 % 2 == 0:
                    s.wait_ge(sV, fv[("norm", tt16)])
                else:
                    s.wait_ge(sA, fa[("norm", tt16)])
                s.dma_start(out=z_h[tt16 * 128:(tt16 + 1) * 128, :],
                            in_=stream[:, tt16 * D:(tt16 + 1) * D]).then_inc(sZ, 16)
            s.wait_ge(sZ, 16 * 16)

        ffn.close()
        af_scope.close()
        a_roll_scope.close()

    right.close()
    left.close()
    sems.close()
    return nc


def _host_layout(inputs):
    f32 = np.float32
    wq = np.ascontiguousarray(inputs["L0_attn_Wq"], dtype=f32)
    wk = np.ascontiguousarray(inputs["L0_attn_Wk"], dtype=f32)
    wv = np.ascontiguousarray(inputs["L0_attn_Wv"], dtype=f32)
    bq = np.asarray(inputs["L0_attn_bq"], dtype=f32)
    bk = np.asarray(inputs["L0_attn_bk"], dtype=f32)
    bv = np.asarray(inputs["L0_attn_bv"], dtype=f32)
    win = np.ascontiguousarray(inputs["L0_ffn_Win"], dtype=f32)
    binp = np.asarray(inputs["L0_ffn_bin_"], dtype=f32)
    w1 = np.asarray(inputs["L0_ffn_W1"], dtype=f32)
    b1 = np.asarray(inputs["L0_ffn_b1"], dtype=f32)
    w2 = np.asarray(inputs["L0_ffn_W2"], dtype=f32)
    b2 = np.asarray(inputs["L0_ffn_b2"], dtype=f32)
    wout = np.ascontiguousarray(inputs["L0_ffn_Wout"], dtype=f32)
    bout = np.asarray(inputs["L0_ffn_bout"], dtype=f32)

    w1e = np.zeros((M, MH), dtype=f32)
    w2eT = np.zeros((MH, M), dtype=f32)
    rows = np.arange(M)
    cols = (np.arange(M) * INNER)[:, None] + np.arange(INNER)[None, :]
    w1e[rows[:, None], cols] = w1
    w2eT[cols, rows[:, None]] = w2

    d = {
        "wq": wq, "wk": wk, "wv": wv,
        "bqT": np.ascontiguousarray(bq.reshape(8, 128).T),
        "bkT": np.ascontiguousarray(bk.reshape(8, 128).T),
        "bvb": np.ascontiguousarray(np.broadcast_to(bv[None, :], (128, D))),
        "win": win,
        "binc": np.ascontiguousarray(binp.reshape(M, 1)),
        "w1e": w1e,
        "b1T": np.ascontiguousarray(b1.reshape(MH).reshape(8, 128).T),
        "w2eT": w2eT,
        "b2c": np.ascontiguousarray(b2.reshape(M, 1)),
        "wout": wout,
        "boutr": np.ascontiguousarray(bout.reshape(1, D)),
        "mask": np.ascontiguousarray(
            np.where(np.arange(128)[None, :] >= np.arange(128)[:, None],
                     np.float32(0.0), np.float32(-1e30))),
    }
    return d


def kernel(**inputs):
    xfull = np.ascontiguousarray(np.asarray(inputs["L0_attn_minibatch"], dtype=np.float32))
    shared = _host_layout(inputs)
    nc = build_nc()
    in_maps = []
    for b in range(8):
        m = dict(shared)
        m["x"] = np.ascontiguousarray(xfull[b])
        m["xt"] = np.ascontiguousarray(xfull[b].T)
        in_maps.append(m)
    res = run_bass_kernel_spmd(nc, in_maps, core_ids=list(range(8)))
    return np.stack([np.asarray(r["z"], dtype=np.float32) for r in res.results], axis=0)
